# version 08 (final)
# baseline (speedup 1.0000x reference)
import contextlib
import numpy as np
import concourse.bass as bass
import concourse.mybir as mybir
from concourse.bass_utils import run_bass_kernel_spmd

F32 = mybir.dt.float32
BF16 = mybir.dt.bfloat16
I32 = mybir.dt.int32
AF = mybir.ActivationFunctionType
OP = mybir.AluOpType
AX = mybir.AxisListType.X

D = 2048
TOK = 2048
HALO = 2176
NT = HALO + TOK
NTH = 2114
SEG = 4096
I_FFN = 8192


def _bd(m):
    a, b = m.shape
    out = np.zeros((2 * a, 2 * b), m.dtype)
    out[:a, :b] = m
    out[a:, b:] = m
    return out


def make_consts():
    n = np.arange(64)
    c64 = np.cos(2 * np.pi * np.outer(n, n) / 64).astype(np.float32)
    s64 = np.sin(2 * np.pi * np.outer(n, n) / 64).astype(np.float32)
    cons = {}
    cons["BDWr"] = _bd(c64)
    cons["BDWi"] = _bd(-s64)
    cons["BDWs"] = _bd(s64)
    cons["BDIc"] = _bd(c64 / 64)
    cons["BDIsn"] = _bd(-s64 / 64)
    cons["BDIs"] = _bd(s64 / 64)
    m = np.arange(32)
    c2 = (np.cos(2 * np.pi * np.outer(n, 32 + m) / 64) / 64).astype(np.float32)
    s2 = (-np.sin(2 * np.pi * np.outer(n, 32 + m) / 64) / 64).astype(np.float32)
    cons["C2H"] = _bd(c2)
    cons["S2H"] = _bd(s2)
    p = np.arange(128) % 64
    col = np.arange(512) % 64
    cons["TWC"] = np.cos(2 * np.pi * np.outer(p, col) / 4096).astype(np.float32)
    cons["TWS"] = np.sin(2 * np.pi * np.outer(p, col) / 4096).astype(np.float32)
    cons["IDN"] = np.eye(128, dtype=np.float32)
    cons["ONES1"] = np.ones((1, 128), np.float32)
    return cons


CONST_BF = ["BDWr", "BDWi", "BDWs", "BDIc", "BDIsn", "BDIs", "C2H", "S2H", "IDN"]
CONST_F32 = ["TWC", "TWS", "ONES1"]


class Sched:
    ENGS = ["sync", "tensor", "vector", "scalar", "gpsimd"]

    def __init__(self):
        self.ops = {e: [] for e in self.ENGS}
        self.cnt = {e: 0 for e in self.ENGS}
        self.cnt["ld"] = 0

    def emit(self, eng, fn, deps=(), dma=False):
        inc = 16 if dma else 1
        self.ops[eng].append((fn, tuple(deps), inc))
        self.cnt[eng] += inc
        return (eng, self.cnt[eng])

    def sig(self):
        self.ops["sync"].append(("SIG", (("sync", self.cnt["sync"]),), 0))
        self.cnt["ld"] += 1
        return ("ld", self.cnt["ld"])

    def dep(self, eng):
        if eng == "sync":
            return self.sig()
        return (eng, self.cnt[eng])

    def barrier(self):
        sl = self.sig()
        marks = dict(self.cnt)
        for e in self.ENGS:
            deps = [(o, marks[o]) for o in self.ENGS if o != e and o != "sync"]
            if e != "sync":
                deps.append(sl)
            self.ops[e].append((None, tuple(deps), 0))


def build(nc):
    dt = nc.dram_tensor
    x_own = dt("x_own", [TOK, D], F32, kind="ExternalInput").ap()
    x_halo = dt("x_halo", [HALO, D], F32, kind="ExternalInput").ap()
    mask1 = dt("mask1", [NT], F32, kind="ExternalInput").ap()
    WqT = dt("WqT", [D, D], F32, kind="ExternalInput").ap()
    WkT = dt("WkT", [D, D], F32, kind="ExternalInput").ap()
    WvT = dt("WvT", [D, D], F32, kind="ExternalInput").ap()
    WoT = dt("WoT", [D, D], F32, kind="ExternalInput").ap()
    bq = dt("bq", [16, 128], F32, kind="ExternalInput").ap()
    bk = dt("bk", [16, 128], F32, kind="ExternalInput").ap()
    bv = dt("bv", [16, 128], F32, kind="ExternalInput").ap()
    bo = dt("bo", [1, D], F32, kind="ExternalInput").ap()
    cqw = dt("cqw", [16, 128, 3], F32, kind="ExternalInput").ap()
    ckw = dt("ckw", [16, 128, 3], F32, kind="ExternalInput").ap()
    cqb = dt("cqb", [128, 16], F32, kind="ExternalInput").ap()
    ckb = dt("ckb", [128, 16], F32, kind="ExternalInput").ap()
    dskip = dt("dskip", [128, 16], F32, kind="ExternalInput").ap()
    ssmk = dt("ssmk", [D, 2048], F32, kind="ExternalInput").ap()
    w_gate = dt("w_gate", [I_FFN, D], F32, kind="ExternalInput").ap()
    w_up = dt("w_up", [I_FFN, D], F32, kind="ExternalInput").ap()
    w_down = dt("w_down", [D, I_FFN], F32, kind="ExternalInput").ap()
    consts = {}
    for nm in CONST_BF:
        shp = [128, 64] if nm in ("C2H", "S2H") else [128, 128]
        consts[nm] = dt(nm, shp, BF16, kind="ExternalInput").ap()
    consts["TWC"] = dt("TWC", [128, 512], F32, kind="ExternalInput").ap()
    consts["TWS"] = dt("TWS", [128, 512], F32, kind="ExternalInput").ap()
    consts["ONES1"] = dt("ONES1", [1, 128], F32, kind="ExternalInput").ap()
    out = dt("out", [TOK, D], F32, kind="ExternalOutput").ap()
    xnT = dt("xnT", [D, NT], BF16, kind="Internal").ap()
    xnqT = dt("xnqT", [D, TOK], BF16, kind="Internal").ap()
    kvseg = dt("kvseg", [D, SEG], BF16, kind="Internal").ap()
    kv_own = dt("kv_own", [D, TOK], F32, kind="Internal").ap()
    q_own = dt("q_own", [D, TOK], F32, kind="Internal").ap()
    ydr = dt("ydr", [D, TOK], F32, kind="Internal").ap()
    ygb = dt("ygb", [D, TOK], BF16, kind="Internal").ap()
    attn = dt("attn", [TOK, D], F32, kind="Internal").ap()
    kbf = dt("kbf", [D, 2048], BF16, kind="Internal").ap()
    wqb = dt("wqb", [D, D], BF16, kind="Internal").ap()
    wkb = dt("wkb", [D, D], BF16, kind="Internal").ap()
    wvb = dt("wvb", [D, D], BF16, kind="Internal").ap()
    wob = dt("wob", [D, D], BF16, kind="Internal").ap()
    wgqT = dt("wgqT", [D, I_FFN], BF16, kind="Internal").ap()
    wuqT = dt("wuqT", [D, I_FFN], BF16, kind="Internal").ap()
    wdqT = dt("wdqT", [I_FFN, D], BF16, kind="Internal").ap()
    borep_d = dt("borep_d", [128, D], F32, kind="Internal").ap()
    mrep_d = dt("mrep_d", [128, NT], F32, kind="Internal").ap()

    es = contextlib.ExitStack()
    S = Sched()
    with es:
        ec = es.enter_context
        ARB = ec(nc.sbuf_tensor("ARB", [128, 49152], BF16))
        ARB2 = ec(nc.sbuf_tensor("ARB2", [128, 10240], BF16))
        ARF = ec(nc.sbuf_tensor("ARF", [128, 12288], F32))
        ARI = ec(nc.sbuf_tensor("ARI", [128, 4224], I32))
        SM = ec(nc.sbuf_tensor("SM", [128, 256], F32))
        CB = {nm: ec(nc.sbuf_tensor("c_" + nm, [128, 64 if nm in ("C2H", "S2H") else 128], BF16)) for nm in CONST_BF}
        TWCs = ec(nc.sbuf_tensor("TWCs", [128, 512], F32))
        TWSs = ec(nc.sbuf_tensor("TWSs", [128, 512], F32))
        cqs = ec(nc.sbuf_tensor("cqs", [128, 48], F32))
        cks = ec(nc.sbuf_tensor("cks", [128, 48], F32))
        cqbs = ec(nc.sbuf_tensor("cqbs", [128, 16], F32))
        ckbs = ec(nc.sbuf_tensor("ckbs", [128, 16], F32))
        dsks = ec(nc.sbuf_tensor("dsks", [128, 16], F32))
        ones_sb = ec(nc.sbuf_tensor("ones_sb", [1, 128], F32))
        brow = ec(nc.sbuf_tensor("brow", [1, 512], F32))
        PSA = ec(nc.psum_tensor("PSA", [128, 1024], F32))
        PSB = ec(nc.psum_tensor("PSB", [128, 1024], F32))
        PSC = ec(nc.psum_tensor("PSC", [128, 1024], F32))
        PST = ec(nc.psum_tensor("PST", [128, 128], BF16))
        sems = {e: ec(nc.semaphore("s_" + e)) for e in Sched.ENGS}
        sems["ld"] = ec(nc.semaphore("s_ld"))
        block = ec(nc.Block())

        def quant(X_f32, Q_bf, F, mfrac, mn, dep):
            NB = F // 32
            am = SM[:, 0:NB]
            EA = ARI[:, 0:NB]
            Exm = ARI[:, 64:64 + F]
            en2 = ARI[:, 64 + F:64 + 2 * F]
            yt = ARF[:, 10240:10240 + F] if F <= 2048 else None
            x3 = X_f32.rearrange("p (b w) -> p b w", w=32)
            S.emit("vector", lambda e: e.tensor_reduce(out=am.unsqueeze(2), in_=x3, axis=AX, op=OP.max, apply_absolute_value=True), deps=dep)
            S.emit("vector", lambda e: e.tensor_scalar(out=am, in0=am, scalar1=1e-30, scalar2=None, op0=OP.max))
            S.emit("vector", lambda e: e.tensor_scalar(out=EA, in0=am.bitcast(I32), scalar1=23, scalar2=None, op0=OP.logical_shift_right))
            S.emit("vector", lambda e: e.tensor_scalar(out=EA, in0=EA, scalar1=2, scalar2=None, op0=OP.subtract))
            S.emit("vector", lambda e: e.tensor_scalar(out=Exm, in0=X_f32.bitcast(I32), scalar1=0x7FFFFFFF, scalar2=23, op0=OP.bitwise_and, op1=OP.logical_shift_right))
            EAb = EA.unsqueeze(2).broadcast_to([128, NB, 32])
            Ex3 = Exm.rearrange("p (b w) -> p b w", w=32)
            S.emit("vector", lambda e: e.tensor_tensor(out=Ex3, in0=Ex3, in1=EAb, op=OP.max))
            S.emit("vector", lambda e: e.tensor_scalar(out=en2, in0=Exm, scalar1=-1, scalar2=254 + mfrac, op0=OP.mult, op1=OP.add))
            S.emit("vector", lambda e: e.tensor_scalar(out=en2, in0=en2, scalar1=23, scalar2=None, op0=OP.logical_shift_left))
            S.emit("vector", lambda e: e.tensor_tensor(out=yt, in0=X_f32, in1=en2.bitcast(F32), op=OP.mult))
            S.emit("vector", lambda e: e.tensor_scalar(out=yt, in0=yt, scalar1=576.5, scalar2=None, op0=OP.add))
            S.emit("vector", lambda e: e.tensor_scalar(out=yt.bitcast(I32), in0=yt.bitcast(I32), scalar1=-16384, scalar2=None, op0=OP.bitwise_and))
            S.emit("vector", lambda e: e.tensor_scalar(out=en2, in0=Exm, scalar1=mfrac, scalar2=None, op0=OP.subtract))
            S.emit("vector", lambda e: e.tensor_scalar(out=en2, in0=en2, scalar1=23, scalar2=None, op0=OP.logical_shift_left))
            S.emit("vector", lambda e: e.scalar_tensor_tensor(out=X_f32, in0=yt, scalar=576.0, in1=en2.bitcast(F32), op0=OP.subtract, op1=OP.mult))
            S.emit("vector", lambda e: e.tensor_scalar(out=EA, in0=EA, scalar1=23, scalar2=None, op0=OP.logical_shift_left))
            bnd = SM[:, 64:64 + NB]
            nbd = SM[:, 128:128 + NB]
            S.emit("vector", lambda e: e.tensor_scalar(out=bnd, in0=EA.bitcast(F32), scalar1=mn, scalar2=None, op0=OP.mult))
            S.emit("vector", lambda e: e.tensor_scalar(out=nbd, in0=bnd, scalar1=-1.0, scalar2=None, op0=OP.mult))
            q3i = X_f32.rearrange("p (b w) -> p b w", w=32)
            q3o = Q_bf.rearrange("p (b w) -> p b w", w=32)
            S.emit("vector", lambda e: e.tensor_tensor(out=q3i, in0=q3i, in1=bnd.unsqueeze(2).broadcast_to([128, NB, 32]), op=OP.min))
            return S.emit("vector", lambda e: e.tensor_tensor(out=q3o, in0=q3i, in1=nbd.unsqueeze(2).broadcast_to([128, NB, 32]), op=OP.max))

        def transp(bf_in, bf_out, dep):
            m1 = S.emit("tensor", lambda e: e.transpose(out=PST[:], in_=bf_in, identity=CB["IDN"][:]), deps=list(dep) + [S.dep("scalar")])
            return S.emit("scalar", lambda e: e.copy(out=bf_out, in_=PST[:]), deps=[m1])

        for nm in CONST_BF:
            S.emit("sync", lambda e, n=nm: e.dma_start(out=CB[n][:], in_=consts[n]), dma=True)
        S.emit("sync", lambda e: e.dma_start(out=TWCs[:], in_=consts["TWC"]), dma=True)
        S.emit("sync", lambda e: e.dma_start(out=TWSs[:], in_=consts["TWS"]), dma=True)
        bqs, bks, bvs = SM[:, 192:208], SM[:, 208:224], SM[:, 224:240]
        for cc in range(16):
            S.emit("sync", lambda e, c=cc: e.dma_start(out=bqs[:, c:c + 1], in_=bq[c, :].unsqueeze(1)), dma=True)
            S.emit("sync", lambda e, c=cc: e.dma_start(out=bks[:, c:c + 1], in_=bk[c, :].unsqueeze(1)), dma=True)
            S.emit("sync", lambda e, c=cc: e.dma_start(out=bvs[:, c:c + 1], in_=bv[c, :].unsqueeze(1)), dma=True)
            S.emit("sync", lambda e, c=cc: e.dma_start(out=cqs[:, 3 * c:3 * c + 3], in_=cqw[c]), dma=True)
            S.emit("sync", lambda e, c=cc: e.dma_start(out=cks[:, 3 * c:3 * c + 3], in_=ckw[c]), dma=True)
        S.emit("sync", lambda e: e.dma_start(out=cqbs[:], in_=cqb), dma=True)
        S.emit("sync", lambda e: e.dma_start(out=ckbs[:], in_=ckb), dma=True)
        S.emit("sync", lambda e: e.dma_start(out=dsks[:], in_=dskip), dma=True)
        S.emit("sync", lambda e: e.dma_start(out=ones_sb[:], in_=consts["ONES1"]), dma=True)
        tmp = ARF[:, 0:512]
        for t in range(4):
            S.emit("sync", lambda e, t=t: e.dma_start(out=brow[:], in_=bo[:, t * 512:(t + 1) * 512]), deps=[S.dep("tensor")], dma=True)
            lb = S.sig()
            m1 = S.emit("tensor", lambda e, t=t: e.matmul(out=PSA[:, 0:512], lhsT=ones_sb[:], rhs=brow[:], start=True, stop=True), deps=[lb, S.dep("vector")])
            m2 = S.emit("vector", lambda e: e.tensor_copy(out=tmp, in_=PSA[:, 0:512]), deps=[m1])
            S.emit("sync", lambda e, t=t: e.dma_start(out=borep_d[:, t * 512:(t + 1) * 512], in_=tmp), deps=[m2], dma=True)
        for t in range(9):
            w = min(512, NT - t * 512)
            S.emit("sync", lambda e, t=t, w=w: e.dma_start(out=brow[:, 0:w], in_=mask1[t * 512:t * 512 + w].unsqueeze(0)), deps=[S.dep("tensor")], dma=True)
            lb = S.sig()
            m1 = S.emit("tensor", lambda e, t=t, w=w: e.matmul(out=PSA[:, 0:w], lhsT=ones_sb[:], rhs=brow[:, 0:w], start=True, stop=True), deps=[lb, S.dep("vector")])
            m2 = S.emit("vector", lambda e, w=w: e.tensor_copy(out=tmp[:, 0:w], in_=PSA[:, 0:w]), deps=[m1, S.dep("sync")])
            S.emit("sync", lambda e, t=t, w=w: e.dma_start(out=mrep_d[:, t * 512:t * 512 + w], in_=tmp[:, 0:w]), deps=[m2], dma=True)
        xf = ARF[:, 1024:3072]
        xb = ARB2[:, 0:2048]
        for (src, dst) in [(WqT, wqb), (WkT, wkb), (WvT, wvb), (WoT, wob), (ssmk, kbf)]:
            for r in range(16):
                S.emit("sync", lambda e, s=src, r=r: e.dma_start(out=xf, in_=s[r * 128:(r + 1) * 128, :]), deps=[S.dep("scalar")], dma=True)
                m1 = S.sig()
                m2 = S.emit("scalar", lambda e: e.copy(out=xb, in_=xf), deps=[m1, S.dep("sync")])
                S.emit("sync", lambda e, d=dst, r=r: e.dma_start(out=d[r * 128:(r + 1) * 128, :], in_=xb), deps=[m2], dma=True)
        S.barrier()

        xtile = ARF[:, 0:2048]
        xnf = ARF[:, 2048:4096]
        sq = ARF[:, 4096:6144]
        stats = SM[:, 240:248]
        xnb = ARB2[:, 0:2048]
        xnq = ARB2[:, 2048:4096]
        for ti in range(33):
            src = x_halo[ti * 128:(ti + 1) * 128, :] if ti < 17 else x_own[(ti - 17) * 128:(ti - 16) * 128, :]
            S.emit("sync", lambda e, s=src: e.dma_start(out=xtile, in_=s), dma=True)
            ld = S.sig()
            S.emit("vector", lambda e: e.tensor_reduce(out=stats[:, 0:1].unsqueeze(2), in_=xtile.unsqueeze(1), axis=AX, op=OP.add), deps=[ld])
            msq = S.emit("scalar", lambda e: e.activation(out=sq, in_=xtile, func=AF.Square), deps=[ld])
            S.emit("vector", lambda e: e.tensor_reduce(out=stats[:, 1:2].unsqueeze(2), in_=sq.unsqueeze(1), axis=AX, op=OP.add), deps=[msq])
            S.emit("vector", lambda e: e.tensor_scalar(out=stats[:, 0:1], in0=stats[:, 0:1], scalar1=1.0 / 2048, scalar2=None, op0=OP.mult))
            S.emit("vector", lambda e: e.tensor_scalar(out=stats[:, 1:2], in0=stats[:, 1:2], scalar1=1.0 / 2048, scalar2=None, op0=OP.mult))
            S.emit("vector", lambda e: e.tensor_tensor(out=stats[:, 2:3], in0=stats[:, 0:1], in1=stats[:, 0:1], op=OP.mult))
            mv = S.emit("vector", lambda e: e.tensor_tensor(out=stats[:, 1:2], in0=stats[:, 1:2], in1=stats[:, 2:3], op=OP.subtract))
            S.emit("vector", lambda e: e.tensor_scalar(out=stats[:, 1:2], in0=stats[:, 1:2], scalar1=1e-5, scalar2=None, op0=OP.add), deps=[mv])
            mv2 = S.dep("vector")
            ms = S.emit("scalar", lambda e: e.activation(out=stats[:, 2:3], in_=stats[:, 1:2], func=AF.Sqrt), deps=[mv2])
            S.emit("vector", lambda e: e.reciprocal(out=stats[:, 3:4], in_=stats[:, 2:3]), deps=[ms])
            mxn = S.emit("vector", lambda e: e.tensor_scalar(out=xnf, in0=xtile, scalar1=stats[:, 0:1], scalar2=stats[:, 3:4], op0=OP.subtract, op1=OP.mult))
            mcb = S.emit("scalar", lambda e: e.copy(out=xnb, in_=xnf), deps=[mxn])
            for hc in range(16):
                mt = transp(xnb[:, hc * 128:(hc + 1) * 128], ARB2[:, 4096 + hc * 128:4096 + (hc + 1) * 128], [mcb])
                S.emit("sync", lambda e, hc=hc, ti=ti: e.dma_start(out=xnT[hc * 128:(hc + 1) * 128, ti * 128:(ti + 1) * 128], in_=ARB2[:, 4096 + hc * 128:4096 + (hc + 1) * 128]), deps=[mt], dma=True)
            if ti >= 17:
                mq = quant(xnf, xnq, 2048, 3, 7.5, [mxn, mcb, S.dep("tensor")])
                for hc in range(16):
                    mt = transp(xnq[:, hc * 128:(hc + 1) * 128], ARB2[:, 6144 + hc * 128:6144 + (hc + 1) * 128], [mq])
                    S.emit("sync", lambda e, hc=hc, ti=ti: e.dma_start(out=xnqT[hc * 128:(hc + 1) * 128, (ti - 17) * 128:(ti - 16) * 128], in_=ARB2[:, 6144 + hc * 128:6144 + (hc + 1) * 128]), deps=[mt], dma=True)
            S.barrier()

        wf = ARF[:, 0:2048]
        wq = ARB2[:, 0:2048]
        for (src, dst) in [(w_gate, wgqT), (w_up, wuqT)]:
            for ic in range(64):
                S.emit("sync", lambda e, s=src, ic=ic: e.dma_start(out=wf, in_=s[ic * 128:(ic + 1) * 128, :]), dma=True)
                ld = S.sig()
                mq = quant(wf, wq, 2048, 1, 6.0, [ld])
                for hc in range(16):
                    mt = transp(wq[:, hc * 128:(hc + 1) * 128], ARB2[:, 4096 + hc * 128:4096 + (hc + 1) * 128], [mq])
                    S.emit("sync", lambda e, d=dst, hc=hc, ic=ic: e.dma_start(out=d[hc * 128:(hc + 1) * 128, ic * 128:(ic + 1) * 128], in_=ARB2[:, 4096 + hc * 128:4096 + (hc + 1) * 128]), deps=[mt], dma=True)
                S.barrier()
        for hcc in range(16):
            for qq in range(4):
                S.emit("sync", lambda e, hcc=hcc, qq=qq: e.dma_start(out=wf, in_=w_down[hcc * 128:(hcc + 1) * 128, qq * 2048:(qq + 1) * 2048]), dma=True)
                ld = S.sig()
                mq = quant(wf, wq, 2048, 1, 6.0, [ld])
                for ic in range(16):
                    mt = transp(wq[:, ic * 128:(ic + 1) * 128], ARB2[:, 4096 + ic * 128:4096 + (ic + 1) * 128], [mq])
                    S.emit("sync", lambda e, hcc=hcc, qq=qq, ic=ic: e.dma_start(out=wdqT[qq * 2048 + ic * 128:qq * 2048 + (ic + 1) * 128, hcc * 128:(hcc + 1) * 128], in_=ARB2[:, 4096 + ic * 128:4096 + (ic + 1) * 128]), deps=[mt], dma=True)
                S.barrier()

        xnTh = ARB[:, 0:16 * NTH]
        lw = ARB[:, 40960:40960 + 3 * 2048]
        ksb = ARF[:, 0:NTH]
        vsb = ARF[:, 2200:2200 + NTH]
        kcsb = ARF[:, 4400:4400 + NTH]
        kc2 = ARF[:, 6600:6600 + NTH]
        mrh = ARF[:, 8800:8800 + NTH]
        kvb = ARB2[:, 0:NTH]
        for half in range(2):
            c0 = 0 if half == 0 else 2110
            lds = []
            for hc in range(16):
                S.emit("sync", lambda e, hc=hc, c0=c0: e.dma_start(out=xnTh[:, hc * NTH:hc * NTH + NTH], in_=xnT[hc * 128:(hc + 1) * 128, c0:c0 + NTH]), dma=True)
                lds.append(S.sig())
            S.emit("sync", lambda e, c0=c0: e.dma_start(out=mrh, in_=mrep_d[:, c0:c0 + NTH]), dma=True)
            lds.append(S.sig())
            for cc in range(16):
                lw_ld = []
                for pi, wsrc in enumerate([wqb, wkb, wvb]):
                    S.emit("sync", lambda e, w=wsrc, cc=cc, pi=pi: e.dma_start(out=lw[:, pi * 2048:(pi + 1) * 2048].rearrange("p (h n) -> p h n", n=128), in_=w.rearrange("(h p) e -> p h e", p=128)[:, :, cc * 128:(cc + 1) * 128]), dma=True)
                    lw_ld.append(S.sig())
                for pi, (dstb, bias) in enumerate([(ksb, bks), (vsb, bvs), (kcsb, bqs)]):
                    li = [1, 2, 0][pi]
                    for sub in range(5):
                        f0 = sub * 512 if sub < 4 else NTH - 128
                        w = 512 if sub < 4 else 128
                        for hc in range(16):
                            dd = (lds + [lw_ld[li], S.dep("vector")]) if hc == 0 else []
                            S.emit("tensor", lambda e, hc=hc, li=li, f0=f0, w=w: e.matmul(out=PSA[:, 0:w], lhsT=lw[:, li * 2048 + hc * 128:li * 2048 + (hc + 1) * 128], rhs=xnTh.rearrange("p (h n) -> p h n", n=NTH)[:, hc, f0:f0 + w], start=(hc == 0), stop=(hc == 15)), deps=dd)
                        mm = S.dep("tensor")
                        S.emit("vector", lambda e, dstb=dstb, bias=bias, cc=cc, f0=f0, w=w: e.tensor_scalar(out=dstb[:, f0:f0 + w], in0=PSA[:, 0:w], scalar1=bias[:, cc:cc + 1], scalar2=None, op0=OP.add), deps=[mm])
                S.emit("vector", lambda e: e.tensor_tensor(out=ksb, in0=ksb, in1=mrh, op=OP.mult))
                S.emit("vector", lambda e: e.tensor_tensor(out=kcsb, in0=kcsb, in1=mrh, op=OP.mult))
                S.emit("vector", lambda e, cc=cc: e.tensor_scalar(out=kc2, in0=ksb, scalar1=cks[:, 3 * cc + 2:3 * cc + 3], scalar2=ckbs[:, cc:cc + 1], op0=OP.mult, op1=OP.add))
                S.emit("vector", lambda e, cc=cc: e.scalar_tensor_tensor(out=kc2[:, 1:NTH], in0=ksb[:, 0:NTH - 1], scalar=cks[:, 3 * cc + 1:3 * cc + 2], in1=kc2[:, 1:NTH], op0=OP.mult, op1=OP.add))
                S.emit("vector", lambda e, cc=cc: e.scalar_tensor_tensor(out=kc2[:, 2:NTH], in0=ksb[:, 0:NTH - 2], scalar=cks[:, 3 * cc:3 * cc + 1], in1=kc2[:, 2:NTH], op0=OP.mult, op1=OP.add))
                mkv = S.emit("vector", lambda e: e.tensor_tensor(out=ksb, in0=kc2, in1=vsb, op=OP.mult))
                mcb2 = S.emit("scalar", lambda e: e.copy(out=kvb, in_=ksb), deps=[mkv])
                S.emit("vector", lambda e, cc=cc: e.tensor_scalar(out=vsb, in0=kcsb, scalar1=cqs[:, 3 * cc + 2:3 * cc + 3], scalar2=cqbs[:, cc:cc + 1], op0=OP.mult, op1=OP.add), deps=[mkv])
                S.emit("vector", lambda e, cc=cc: e.scalar_tensor_tensor(out=vsb[:, 1:NTH], in0=kcsb[:, 0:NTH - 1], scalar=cqs[:, 3 * cc + 1:3 * cc + 2], in1=vsb[:, 1:NTH], op0=OP.mult, op1=OP.add))
                mq2 = S.emit("vector", lambda e, cc=cc: e.scalar_tensor_tensor(out=vsb[:, 2:NTH], in0=kcsb[:, 0:NTH - 2], scalar=cqs[:, 3 * cc:3 * cc + 1], in1=vsb[:, 2:NTH], op0=OP.mult, op1=OP.add))
                r0, r1 = cc * 128, (cc + 1) * 128
                if half == 0:
                    S.emit("sync", lambda e, r0=r0, r1=r1: e.dma_start(out=kvseg[r0:r1, 0:1984], in_=kvb[:, 128:2112]), deps=[mcb2], dma=True)
                else:
                    S.emit("sync", lambda e, r0=r0, r1=r1: e.dma_start(out=kvseg[r0:r1, 1984:4096], in_=kvb[:, 2:2114]), deps=[mcb2], dma=True)
                    S.emit("sync", lambda e, r0=r0, r1=r1: e.dma_start(out=kv_own[r0:r1, :], in_=ksb[:, 66:2114]), deps=[mcb2], dma=True)
                    S.emit("sync", lambda e, r0=r0, r1=r1: e.dma_start(out=q_own[r0:r1, :], in_=vsb[:, 66:2114]), deps=[mq2], dma=True)
                S.barrier()

        rhsK = ARB2[:, 0:512]
        rhsV = ARB2[:, 512:1024]
        TrTi = ARB2[:, 1024:2048]
        TrTw = ARB2[:, 2048:3072]
        PrPi = ARB2[:, 3072:4096]
        T2 = ARB2[:, 4096:5120]
        T2T = ARB2[:, 5120:6144]
        Kf = ARB[:, 0:8192]
        e1 = ARF[:, 0:512]
        e2 = ARF[:, 512:1024]
        ysb = ARF[:, 1024:1536]
        S.emit("vector", lambda e: e.memset(rhsK[:], 0.0))
        S.barrier()

        def fwd_stage(rhs_tile, kdep):
            m1 = S.emit("tensor", lambda e: e.matmul(out=PSA[:, 0:512], lhsT=CB["BDWr"][:], rhs=rhs_tile, start=True, stop=True), deps=list(kdep) + [S.dep("vector")])
            m2 = S.emit("tensor", lambda e: e.matmul(out=PSB[:, 0:512], lhsT=CB["BDWi"][:], rhs=rhs_tile, start=True, stop=True))
            S.emit("vector", lambda e: e.tensor_tensor(out=e1, in0=PSA[:, 0:512], in1=TWCs[:], op=OP.mult), deps=[m1, m2, S.dep("tensor")])
            S.emit("vector", lambda e: e.tensor_tensor(out=e2, in0=PSB[:, 0:512], in1=TWSs[:], op=OP.mult))
            S.emit("vector", lambda e: e.tensor_tensor(out=TrTi[:, 0:512], in0=e1, in1=e2, op=OP.add))
            S.emit("vector", lambda e: e.tensor_tensor(out=e1, in0=PSB[:, 0:512], in1=TWCs[:], op=OP.mult))
            S.emit("vector", lambda e: e.tensor_tensor(out=e2, in0=PSA[:, 0:512], in1=TWSs[:], op=OP.mult))
            mtw = S.emit("vector", lambda e: e.tensor_tensor(out=TrTi[:, 512:1024], in0=e1, in1=e2, op=OP.subtract))
            tms = []
            for s in range(8):
                tms.append(transp(TrTi[:, s * 128:(s + 1) * 128], TrTw[:, s * 128:(s + 1) * 128], [mtw]))
            S.emit("tensor", lambda e: e.matmul(out=PSC[:, 0:512], lhsT=CB["BDWr"][:], rhs=TrTw[:, 0:512], start=True, stop=False), deps=tms)
            S.emit("tensor", lambda e: e.matmul(out=PSC[:, 0:512], lhsT=CB["BDWs"][:], rhs=TrTw[:, 512:1024], start=False, stop=True))
            S.emit("tensor", lambda e: e.matmul(out=PSC[:, 512:1024], lhsT=CB["BDWr"][:], rhs=TrTw[:, 512:1024], start=True, stop=False))
            return S.emit("tensor", lambda e: e.matmul(out=PSC[:, 512:1024], lhsT=CB["BDWi"][:], rhs=TrTw[:, 0:512], start=False, stop=True))

        kbf_r = kbf.rearrange("s (n2 n1) -> n2 s n1", n1=64)
        kvseg_r = kvseg.rearrange("s (n2 n1) -> n2 s n1", n1=64)
        for st in range(16):
            for cs in range(8):
                ch0 = st * 128 + cs * 8
                S.emit("sync", lambda e, ch0=ch0: e.dma_start(out=rhsK[0:32, :].rearrange("p (c n) -> p c n", n=64), in_=kbf_r[0:32, ch0:ch0 + 8, :]), deps=[S.dep("tensor")], dma=True)
                l1 = S.sig()
                S.emit("sync", lambda e, ch0=ch0: e.dma_start(out=rhsK[64:96, :].rearrange("p (c n) -> p c n", n=64), in_=kbf_r[0:32, ch0 + 64:ch0 + 72, :]), dma=True)
                l2 = S.sig()
                mb = fwd_stage(rhsK, [l1, l2, S.dep("scalar")])
                S.emit("scalar", lambda e, cs=cs: e.copy(out=Kf[:, cs * 1024:cs * 1024 + 512], in_=PSC[:, 0:512]), deps=[mb])
                S.emit("scalar", lambda e, cs=cs: e.copy(out=Kf[:, cs * 1024 + 512:cs * 1024 + 1024], in_=PSC[:, 512:1024]))
            for cs in range(8):
                ch0 = st * 128 + cs * 8
                S.emit("sync", lambda e, ch0=ch0: e.dma_start(out=rhsV[0:64, :].rearrange("p (c n) -> p c n", n=64), in_=kvseg_r[:, ch0:ch0 + 8, :]), deps=[S.dep("tensor")], dma=True)
                l1 = S.sig()
                S.emit("sync", lambda e, ch0=ch0: e.dma_start(out=rhsV[64:128, :].rearrange("p (c n) -> p c n", n=64), in_=kvseg_r[:, ch0 + 64:ch0 + 72, :]), dma=True)
                l2 = S.sig()
                mb = fwd_stage(rhsV, [l1, l2, S.dep("scalar")])
                kfr = Kf[:, cs * 1024:cs * 1024 + 512]
                kfi = Kf[:, cs * 1024 + 512:cs * 1024 + 1024]
                S.emit("vector", lambda e, kfr=kfr: e.tensor_tensor(out=e1, in0=PSC[:, 0:512], in1=kfr, op=OP.mult), deps=[mb])
                S.emit("vector", lambda e, kfi=kfi: e.tensor_tensor(out=e2, in0=PSC[:, 512:1024], in1=kfi, op=OP.mult))
                S.emit("vector", lambda e: e.tensor_tensor(out=PrPi[:, 0:512], in0=e1, in1=e2, op=OP.subtract))
                S.emit("vector", lambda e, kfi=kfi: e.tensor_tensor(out=e1, in0=PSC[:, 0:512], in1=kfi, op=OP.mult))
                S.emit("vector", lambda e, kfr=kfr: e.tensor_tensor(out=e2, in0=PSC[:, 512:1024], in1=kfr, op=OP.mult))
                mp = S.emit("vector", lambda e: e.tensor_tensor(out=PrPi[:, 512:1024], in0=e1, in1=e2, op=OP.add))
                i1 = S.emit("tensor", lambda e: e.matmul(out=PSA[:, 0:512], lhsT=CB["BDIc"][:], rhs=PrPi[:, 0:512], start=True, stop=False), deps=[mp])
                i2 = S.emit("tensor", lambda e: e.matmul(out=PSA[:, 0:512], lhsT=CB["BDIsn"][:], rhs=PrPi[:, 512:1024], start=False, stop=True))
                i3 = S.emit("tensor", lambda e: e.matmul(out=PSB[:, 0:512], lhsT=CB["BDIs"][:], rhs=PrPi[:, 0:512], start=True, stop=False))
                i4 = S.emit("tensor", lambda e: e.matmul(out=PSB[:, 0:512], lhsT=CB["BDIc"][:], rhs=PrPi[:, 512:1024], start=False, stop=True))
                S.emit("vector", lambda e: e.tensor_tensor(out=e1, in0=PSA[:, 0:512], in1=TWCs[:], op=OP.mult), deps=[i2, i4])
                S.emit("vector", lambda e: e.tensor_tensor(out=e2, in0=PSB[:, 0:512], in1=TWSs[:], op=OP.mult))
                S.emit("vector", lambda e: e.tensor_tensor(out=T2[:, 0:512], in0=e1, in1=e2, op=OP.subtract))
                S.emit("vector", lambda e: e.tensor_tensor(out=e1, in0=PSA[:, 0:512], in1=TWSs[:], op=OP.mult))
                S.emit("vector", lambda e: e.tensor_tensor(out=e2, in0=PSB[:, 0:512], in1=TWCs[:], op=OP.mult))
                mt2 = S.emit("vector", lambda e: e.tensor_tensor(out=T2[:, 512:1024], in0=e1, in1=e2, op=OP.add))
                tms = []
                for s in range(8):
                    tms.append(transp(T2[:, s * 128:(s + 1) * 128], T2T[:, s * 128:(s + 1) * 128], [mt2]))
                S.emit("tensor", lambda e: e.matmul(out=PSC[0:64, 0:512], lhsT=CB["C2H"][:], rhs=T2T[:, 0:512], start=True, stop=False), deps=tms + [S.dep("vector")])
                y2 = S.emit("tensor", lambda e: e.matmul(out=PSC[0:64, 0:512], lhsT=CB["S2H"][:], rhs=T2T[:, 512:1024], start=False, stop=True))
                my = S.emit("vector", lambda e: e.tensor_copy(out=ysb[0:64, :], in_=PSC[0:64, 0:512]), deps=[y2, S.dep("sync")])
                for g in range(2):
                    for s2 in range(4):
                        sigb = st * 128 + g * 64 + cs * 8 + s2 * 2
                        S.emit("sync", lambda e, g=g, s2=s2, sigb=sigb: e.dma_start(out=ydr[sigb:sigb + 2, :].rearrange("s (m n) -> m s n", n=64), in_=ysb[g * 32:(g + 1) * 32, s2 * 128:(s2 + 1) * 128].rearrange("m (s n) -> m s n", n=64)), deps=[my], dma=True)
            S.barrier()

        yt_ = ARF[:, 0:2048]
        kvt = ARF[:, 2048:4096]
        qt_ = ARF[:, 4096:6144]
        ygt = ARB2[:, 0:2048]
        for cc in range(16):
            r0, r1 = cc * 128, (cc + 1) * 128
            S.emit("sync", lambda e, r0=r0: e.dma_start(out=yt_, in_=ydr[r0:r0 + 128, :]), dma=True)
            l1 = S.sig()
            S.emit("sync", lambda e, r0=r0: e.dma_start(out=kvt, in_=kv_own[r0:r0 + 128, :]), dma=True)
            l2 = S.sig()
            S.emit("sync", lambda e, r0=r0: e.dma_start(out=qt_, in_=q_own[r0:r0 + 128, :]), dma=True)
            l3 = S.sig()
            S.emit("vector", lambda e, cc=cc: e.tensor_scalar(out=kvt, in0=kvt, scalar1=dsks[:, cc:cc + 1], scalar2=None, op0=OP.mult), deps=[l1, l2, l3])
            S.emit("vector", lambda e: e.tensor_tensor(out=yt_, in0=yt_, in1=kvt, op=OP.add))
            mg = S.emit("vector", lambda e: e.tensor_tensor(out=ygt, in0=yt_, in1=qt_, op=OP.mult))
            S.emit("sync", lambda e, r0=r0: e.dma_start(out=ygb[r0:r0 + 128, :], in_=ygt), deps=[mg], dma=True)
            S.barrier()
        WoTb_sb = ARB[:, 0:32768]
        ygcol = ARB[:, 32768:32768 + 2048]
        asb = ARF[:, 0:2048]
        brep = ARF[:, 2048:4096]
        S.emit("sync", lambda e: e.dma_start(out=brep, in_=borep_d), dma=True)
        wols = [S.sig()]
        for hc in range(16):
            S.emit("sync", lambda e, hc=hc: e.dma_start(out=WoTb_sb[:, hc * 2048:(hc + 1) * 2048], in_=wob[hc * 128:(hc + 1) * 128, :]), dma=True)
            wols.append(S.sig())
        ygb_r = ygb.rearrange("(h p) t -> p h t", p=128)
        for tc in range(16):
            S.emit("sync", lambda e, tc=tc: e.dma_start(out=ygcol.rearrange("p (h n) -> p h n", n=128), in_=ygb_r[:, :, tc * 128:(tc + 1) * 128]), deps=[S.dep("tensor")], dma=True)
            lg = S.sig()
            for et in range(4):
                for hc in range(16):
                    dd = ([lg] + wols + [S.dep("vector")]) if hc == 0 else []
                    S.emit("tensor", lambda e, et=et, hc=hc: e.matmul(out=PSA[:, 0:512], lhsT=ygcol[:, hc * 128:(hc + 1) * 128], rhs=WoTb_sb[:, hc * 2048 + et * 512:hc * 2048 + (et + 1) * 512], start=(hc == 0), stop=(hc == 15)), deps=dd)
                mm = S.dep("tensor")
                S.emit("vector", lambda e, et=et: e.tensor_tensor(out=asb[:, et * 512:(et + 1) * 512], in0=PSA[:, 0:512], in1=brep[:, et * 512:(et + 1) * 512], op=OP.add), deps=[mm, S.dep("sync")])
            mo = S.dep("vector")
            S.emit("sync", lambda e, tc=tc: e.dma_start(out=attn[tc * 128:(tc + 1) * 128, :], in_=asb), deps=[mo], dma=True)
        S.barrier()

        wgt = ARB[:, 0:16384]
        wut = ARB[:, 16384:32768]
        wdt = ARB[:, 32768:49152]
        xnqg = ARB2[:, 0:8192]
        iqb = ARB2[:, 8192:9216]
        iqT = ARB2[:, 9216:10240]
        accv = ARF[:, 0:8192]
        gsb = ARF[:, 8192:9216]
        isb = ARF[:, 9216:10240]
        osb = ARF[:, 11264:12288]
        xnqT_r = xnqT.rearrange("(h p) t -> p h t", p=128)
        wgqT_r = wgqT.rearrange("(h p) i -> p h i", p=128)
        wuqT_r = wuqT.rearrange("(h p) i -> p h i", p=128)
        wdqT_r = wdqT.rearrange("(c p) h -> p c h", p=128)
        for grp in range(4):
            S.emit("sync", lambda e, grp=grp: e.dma_start(out=xnqg.rearrange("p (h n) -> p h n", n=512), in_=xnqT_r[:, :, grp * 512:(grp + 1) * 512]), deps=[S.dep("tensor")], dma=True)
            lx = S.sig()
            mzs = []
            for tc in range(4):
                mzs.append(S.emit("vector", lambda e, tc=tc: e.memset(accv[:, tc * 2048:(tc + 1) * 2048], 0.0), deps=[lx]))
            for it in range(8):
                S.emit("sync", lambda e, it=it: e.dma_start(out=wgt.rearrange("p (h n) -> p h n", n=1024), in_=wgqT_r[:, :, it * 1024:(it + 1) * 1024]), deps=[S.dep("tensor")], dma=True)
                lw1 = S.sig()
                S.emit("sync", lambda e, it=it: e.dma_start(out=wut.rearrange("p (h n) -> p h n", n=1024), in_=wuqT_r[:, :, it * 1024:(it + 1) * 1024]), dma=True)
                lw2 = S.sig()
                S.emit("sync", lambda e, it=it: e.dma_start(out=wdt.rearrange("p (c n) -> p c n", n=2048), in_=wdqT_r[:, it * 8:(it + 1) * 8, :]), dma=True)
                lw3 = S.sig()
                for tc in range(4):
                    for s5 in range(2):
                        for hc in range(16):
                            dd = ([lw1, lw2, lw3, S.dep("vector"), S.dep("scalar")] + mzs) if (hc == 0 and s5 == 0) else []
                            S.emit("tensor", lambda e, hc=hc, tc=tc, s5=s5: e.matmul(out=PSA[:, s5 * 512:(s5 + 1) * 512], lhsT=xnqg[:, hc * 512 + tc * 128:hc * 512 + (tc + 1) * 128], rhs=wgt[:, hc * 1024 + s5 * 512:hc * 1024 + (s5 + 1) * 512], start=(hc == 0), stop=(hc == 15)), deps=dd)
                    mg1 = S.dep("tensor")
                    for s5 in range(2):
                        for hc in range(16):
                            S.emit("tensor", lambda e, hc=hc, tc=tc, s5=s5: e.matmul(out=PSB[:, s5 * 512:(s5 + 1) * 512], lhsT=xnqg[:, hc * 512 + tc * 128:hc * 512 + (tc + 1) * 128], rhs=wut[:, hc * 1024 + s5 * 512:hc * 1024 + (s5 + 1) * 512], start=(hc == 0), stop=(hc == 15)))
                    mg2 = S.dep("tensor")
                    ms = S.emit("scalar", lambda e: e.activation(out=gsb, in_=PSA[:, 0:1024], func=AF.Silu), deps=[mg1])
                    mi = S.emit("vector", lambda e: e.tensor_tensor(out=isb, in0=PSB[:, 0:1024], in1=gsb, op=OP.mult), deps=[mg2, ms])
                    mq = quant(isb, iqb, 1024, 3, 7.5, [mi, S.dep("tensor")])
                    tms = []
                    for s in range(8):
                        tms.append(transp(iqb[:, s * 128:(s + 1) * 128], iqT[:, s * 128:(s + 1) * 128], [mq]))
                    for hh in range(4):
                        pz = (hh % 2) * 512
                        for ic in range(8):
                            dd = (tms + [S.dep("vector")]) if ic == 0 else []
                            S.emit("tensor", lambda e, ic=ic, hh=hh, pz=pz: e.matmul(out=PSC[:, pz:pz + 512], lhsT=iqT[:, ic * 128:(ic + 1) * 128], rhs=wdt[:, ic * 2048 + hh * 512:ic * 2048 + (hh + 1) * 512], start=(ic == 0), stop=(ic == 7)), deps=dd)
                        md = S.dep("tensor")
                        S.emit("vector", lambda e, tc=tc, hh=hh, pz=pz: e.tensor_tensor(out=accv[:, tc * 2048 + hh * 512:tc * 2048 + (hh + 1) * 512], in0=accv[:, tc * 2048 + hh * 512:tc * 2048 + (hh + 1) * 512], in1=PSC[:, pz:pz + 512], op=OP.add), deps=[md])
            for tc in range(4):
                gt = grp * 4 + tc
                S.emit("sync", lambda e, gt=gt: e.dma_start(out=gsb, in_=x_own[gt * 128:(gt + 1) * 128, 0:1024]), deps=[S.dep("vector"), S.dep("scalar"), S.sig()], dma=True)
                l1 = S.sig()
                S.emit("sync", lambda e, gt=gt: e.dma_start(out=isb, in_=x_own[gt * 128:(gt + 1) * 128, 1024:2048]), dma=True)
                l2 = S.sig()
                S.emit("sync", lambda e, gt=gt: e.dma_start(out=osb, in_=attn[gt * 128:(gt + 1) * 128, 0:1024]), dma=True)
                l3 = S.sig()
                S.emit("sync", lambda e, gt=gt: e.dma_start(out=ARF[:, 10240:11264], in_=attn[gt * 128:(gt + 1) * 128, 1024:2048]), dma=True)
                l4 = S.sig()
                S.emit("vector", lambda e: e.tensor_tensor(out=gsb, in0=gsb, in1=osb, op=OP.add), deps=[l1, l2, l3, l4])
                S.emit("vector", lambda e, tc=tc: e.tensor_tensor(out=gsb, in0=gsb, in1=accv[:, tc * 2048:tc * 2048 + 1024], op=OP.add))
                m1 = S.dep("vector")
                S.emit("vector", lambda e: e.tensor_tensor(out=isb, in0=isb, in1=ARF[:, 10240:11264], op=OP.add))
                S.emit("vector", lambda e, tc=tc: e.tensor_tensor(out=isb, in0=isb, in1=accv[:, tc * 2048 + 1024:tc * 2048 + 2048], op=OP.add))
                m2 = S.dep("vector")
                S.emit("sync", lambda e, gt=gt: e.dma_start(out=out[gt * 128:(gt + 1) * 128, 0:1024], in_=gsb), deps=[m1], dma=True)
                S.emit("sync", lambda e, gt=gt: e.dma_start(out=out[gt * 128:(gt + 1) * 128, 1024:2048], in_=isb), deps=[m2], dma=True)

        def replay(eng_obj, name):
            done = 0
            serial = name in ("vector", "gpsimd", "scalar")
            for (fn, deps, inc) in S.ops[name]:
                for (dn, dv) in deps:
                    eng_obj.wait_ge(sems[dn], dv)
                if fn is None:
                    continue
                if fn == "SIG":
                    eng_obj.sem_inc(sems["ld"], 1)
                    continue
                if serial:
                    eng_obj.wait_ge(sems[name], done)
                r = fn(eng_obj)
                r.then_inc(sems[name], inc)
                done += inc
            if name == "sync":
                eng_obj.wait_ge(sems["sync"], done)

        @block.sync
        def _(sync):
            replay(sync, "sync")

        @block.tensor
        def _(tensor):
            replay(tensor, "tensor")

        @block.vector
        def _(vector):
            replay(vector, "vector")

        @block.scalar
        def _(scalar):
            replay(scalar, "scalar")

        @block.gpsimd
        def _(gpsimd):
            replay(gpsimd, "gpsimd")

    return nc


def kernel(**inp):
    u = np.asarray(inp["L0_attn_u"], np.float32)
    cons = make_consts()
    nc = bass.Bass("TRN2", target_bir_lowering=False)
    build(nc)
    import ml_dtypes
    base = {
        "WqT": np.ascontiguousarray(np.asarray(inp["L0_attn_Wq"], np.float32).T),
        "WkT": np.ascontiguousarray(np.asarray(inp["L0_attn_Wk"], np.float32).T),
        "WvT": np.ascontiguousarray(np.asarray(inp["L0_attn_Wv"], np.float32).T),
        "WoT": np.ascontiguousarray(np.asarray(inp["L0_attn_Wo"], np.float32).T),
        "bq": np.asarray(inp["L0_attn_bq"], np.float32).reshape(16, 128),
        "bk": np.asarray(inp["L0_attn_bk"], np.float32).reshape(16, 128),
        "bv": np.asarray(inp["L0_attn_bv"], np.float32).reshape(16, 128),
        "bo": np.asarray(inp["L0_attn_bo"], np.float32).reshape(1, D),
        "cqw": np.asarray(inp["L0_attn_convq_w"], np.float32).reshape(16, 128, 3),
        "ckw": np.asarray(inp["L0_attn_convk_w"], np.float32).reshape(16, 128, 3),
        "cqb": np.ascontiguousarray(np.asarray(inp["L0_attn_convq_b"], np.float32).reshape(16, 128).T),
        "ckb": np.ascontiguousarray(np.asarray(inp["L0_attn_convk_b"], np.float32).reshape(16, 128).T),
        "dskip": np.ascontiguousarray(np.asarray(inp["L0_attn_D_skip"], np.float32).reshape(16, 128).T),
        "ssmk": np.asarray(inp["L0_attn_ssm_kernel"], np.float32),
        "w_gate": np.asarray(inp["L0_ffn_w_gate"], np.float32),
        "w_up": np.asarray(inp["L0_ffn_w_up"], np.float32),
        "w_down": np.asarray(inp["L0_ffn_w_down"], np.float32),
    }
    for nm in CONST_BF:
        base[nm] = cons[nm].astype(ml_dtypes.bfloat16)
    for nm in CONST_F32:
        base[nm] = cons[nm]
    in_maps = []
    for c in range(8):
        b, j = c // 4, c % 4
        m = dict(base)
        m["x_own"] = np.ascontiguousarray(u[b, j * TOK:(j + 1) * TOK, :])
        avail = min(j * TOK, HALO)
        pad = HALO - avail
        xh = np.zeros((HALO, D), np.float32)
        if avail:
            xh[pad:] = u[b, j * TOK - avail:j * TOK, :]
        m["x_halo"] = xh
        mk = np.ones(NT, np.float32)
        mk[:pad] = 0.0
        m["mask1"] = mk
        in_maps.append(m)
    res = run_bass_kernel_spmd(nc, in_maps, core_ids=list(range(8)))
    outp = np.empty((2, 8192, D), np.float32)
    for c in range(8):
        b, j = c // 4, c % 4
        outp[b, j * TOK:(j + 1) * TOK, :] = res.results[c]["out"]
    return outp
